# version 05
# baseline (speedup 1.0000x reference)
import numpy as np
import concourse.bass as bass
import concourse.mybir as mybir
from concourse.bass_utils import run_bass_kernel_spmd

F32 = mybir.dt.float32
BF16 = mybir.dt.bfloat16
AF = mybir.ActivationFunctionType
ALU = mybir.AluOpType
AX = mybir.AxisListType

D = 1024
C = D // 128
R = 256
RC = R // 128
T = 4096
TT = 512
NTILE = T // TT
NE = 8
EPS = 1e-5
BIG = 1e30

ENGINES = ("tensor", "vector", "scalar", "sync", "gpsimd")


def build_nc():
    nc = bass.Bass("TRN2", target_bir_lowering=False)

    def din(name, shape, dt=F32):
        return nc.dram_tensor(name, shape, dt, kind="ExternalInput").ap()

    XF = din("XF", [D, T])
    WqT = din("WqT", [D, D]); WkT = din("WkT", [D, D]); WoT = din("WoT", [D, D])
    Vb = din("Vb", [D, R]); Wb = din("Wb", [D, R])
    Vt = din("Vt", [D, R]); Wt = din("Wt", [D, R]); Xt = din("Xt", [D, R])
    UbT = din("UbT", [R, D]); UtT = din("UtT", [R, D])
    BIASB = din("BIASB", [128, C]); BIAST = din("BIAST", [128, C])
    ALPHA = din("ALPHA", [1, 1])
    GWT = din("GWT", [D, NE]); GBIAS = din("GBIAS", [1, NE])
    W1T = din("W1T", [NE * D, D]); W3T = din("W3T", [NE * D, D]); W2T = din("W2T", [NE * D, D])
    SW1T = din("SW1T", [D, 2 * D]); SW3T = din("SW3T", [D, 2 * D]); SW2T = din("SW2T", [2 * D, D])
    IC = din("IC", [1, T])
    IDENT = din("IDENT", [128, 128])
    SELE = din("SELE", [NE, NE * 128])
    YF = nc.dram_tensor("YF", [D, T], F32, kind="ExternalOutput").ap()

    NUNIT = 60
    UHI = nc.dram_tensor("UHI", [NUNIT, 128, 4096], BF16, kind="Internal").ap()
    ULO = nc.dram_tensor("ULO", [NUNIT, 128, 4096], BF16, kind="Internal").ap()

    def unit_idx(c, which, half):
        if c < 2:
            return {1: 0, 3: 1, 2: 2}[which] * 4 + c * 2 + half
        return 12 + (c - 2) * 6 + {1: 0, 3: 1, 2: 2}[which] * 2 + half

    def sb(name, shape, dt=F32):
        return nc.alloc_sbuf_tensor(name, shape, dt)

    xbuf = sb("xbuf", [128, C, TT])
    qbuf = sb("qbuf", [128, C, TT])
    kbuf = sb("kbuf", [128, C, TT])
    r1buf = sb("r1buf", [128, C, TT])
    h1buf = sb("h1buf", [128, C, TT])
    ab = sb("ab", [128, RC, TT]); at = sb("at", [128, RC, TT])
    wb_s = sb("wb_s", [128, RC, TT]); wt_s = sb("wt_s", [128, RC, TT]); uz = sb("uz", [128, RC, TT])
    cb = sb("cb", [128, RC, TT]); ct = sb("ct", [128, RC, TT]); cz = sb("cz", [128, RC, TT])
    pb = sb("pb", [128, RC, TT]); pt = sb("pt", [128, RC, TT])
    wbuf = sb("wbuf", [128, 2, 4096])
    icb = sb("icb", [128, TT])
    geb = sb("geb", [128, 2, TT])
    rb = sb("rb", [128, TT]); mb = sb("mb", [128, TT])
    tmpc = sb("tmpc", [128, 4, TT])
    g_sb = sb("g_sb", [NE, TT]); gf_sb = sb("gf_sb", [NE, TT])
    sc_tm = sb("sc_tm", [128, 32]); ex_tm = sb("ex_tm", [128, 32])
    vv = sb("vv", [128, 32]); ge1 = sb("ge1", [128, 32]); bigb = sb("bigb", [128, 32])
    keep = sb("keep", [128, 32]); gts = sb("gts", [128, 32])
    mx4 = sb("mx4", [128, 4]); nmx4 = sb("nmx4", [128, 4]); sm4 = sb("sm4", [128, 4])
    rs4 = sb("rs4", [128, 4]); m14 = sb("m14", [128, 4]); m24 = sb("m24", [128, 4])
    stats = sb("stats", [1, 2048])
    ic_t = sb("ic_t", [1, TT])
    ident = sb("ident", [128, 128])
    ones_col = sb("ones_col", [128, 1]); ones_row = sb("ones_row", [1, 128])
    ones_scan = sb("ones_scan", [128, TT], BF16)
    fenceb = sb("fenceb", [128, 1])
    alpha_sb = sb("alpha_sb", [1, 1]); alphaB = sb("alphaB", [128, 1])
    biasb_sb = sb("biasb_sb", [128, C]); biast_sb = sb("biast_sb", [128, C])
    bvec = sb("bvec", [128, C]); tmp8 = sb("tmp8", [128, C])
    gb_row = sb("gb_row", [1, NE]); gbB = sb("gbB", [128, NE])
    sele_sb = sb("sele_sb", [NE, NE * 128])
    wob = sb("wob", [128, C])
    carry = sb("carry", [128, 6])

    ST_A, ST_B, ST_C, ST_D_ = 0, 512, 1024, 1536

    ps = [nc.alloc_psum_tensor(f"ps{i}", [128, TT], F32) for i in range(8)]

    sems = {}
    for s_ in ("sLD", "sST", "sT", "sV", "sS"):
        sems[s_] = nc.alloc_semaphore(s_)
    cnt = {s_: 0 for s_ in sems}
    stream = {e: [] for e in ENGINES}
    bank_free = [("sS", 0) for _ in range(8)]

    def op(eng, fn, inc=None, amt=1):
        if inc is None:
            stream[eng].append(fn)
            return None
        cnt[inc] += amt
        sem = sems[inc]
        def wrapped(e, fn=fn, sem=sem, amt=amt):
            fn(e).then_inc(sem, amt)
        stream[eng].append(wrapped)
        return cnt[inc]

    def wait(eng, s_, val):
        if val is None or val <= 0:
            return
        assert val <= cnt[s_], f"wait {s_}>{cnt[s_]} on {eng} for {val}"
        sem = sems[s_]
        stream[eng].append(lambda e, sem=sem, val=val: e.wait_ge(sem, val))

    def waits_merged(eng, waits):
        w = {}
        for (s_, v) in waits:
            if v:
                w[s_] = max(w.get(s_, 0), v)
        for s_, v in w.items():
            wait(eng, s_, v)

    def vhard(fn):
        n = op("vector", fn, inc="sV")
        wait("vector", "sV", n)
        return n

    def dma(eng, out_ap, in_ap, inc, waits=()):
        waits_merged(eng, waits)
        return op(eng, lambda e, o=out_ap, i=in_ap: e.dma_start(out=o, in_=i), inc=inc, amt=16)

    def mm_phase(groups, waits=(), banks=None):
        if banks is None:
            banks = list(range(8))
        assert len(groups) <= len(banks)
        allw = list(waits) + [bank_free[banks[i]] for i in range(len(groups))]
        waits_merged("tensor", allw)
        marks = []
        for gi, g in enumerate(groups):
            bank = ps[banks[gi]]
            out_ap = g["out"](bank) if "out" in g else bank[:]
            mmlist = g["mm"]
            n = len(mmlist)
            for j, (l, r) in enumerate(mmlist):
                if j == n - 1:
                    mk = op("tensor",
                            lambda e, o=out_ap, l=l, r=r, st=(j == 0): e.matmul(o, l, r, start=st, stop=True),
                            inc="sT")
                else:
                    op("tensor",
                       lambda e, o=out_ap, l=l, r=r, st=(j == 0): e.matmul(o, l, r, start=st, stop=False))
            marks.append(mk)
        return marks

    def evict(dst_ap, bank, mark, psum_ap=None, eng="scalar", waits=()):
        waits_merged(eng, [("sT", mark)] + list(waits))
        p = ps[bank][:] if psum_ap is None else psum_ap
        if eng == "scalar":
            n = op(eng, lambda e, d=dst_ap, p=p: e.copy(d, p), inc="sS")
            bank_free[bank] = ("sS", n)
        else:
            n = op(eng, lambda e, d=dst_ap, p=p: e.tensor_copy(d, p), inc="sV")
            bank_free[bank] = ("sV", n)
        return n

    slot_f32 = [wbuf[:, s_, :] for s_ in range(2)]
    slot_bf = [wbuf[:, s_, :].bitcast(BF16) for s_ in range(2)]
    slot_use = [[("sT", 0)], [("sT", 0)]]
    slot_ld = [0, 0]
    slot_next = [0]

    def wload(entries, extra_waits=()):
        s_ = slot_next[0]
        slot_next[0] = 1 - s_
        waits_merged("sync", list(slot_use[s_]) + list(extra_waits))
        mkk = 0
        for dst_fn, src in entries:
            mkk = dma("sync", dst_fn(s_), src, inc="sLD")
        slot_ld[s_] = mkk
        return s_

    def slot_mark(s_, mark, extra=None):
        slot_use[s_] = [("sT", mark)] + (list(extra) if extra else [])

    def v_f32(nchunk, width):
        return lambda s_, n=nchunk, w=width: slot_f32[s_][:, 0:n * w].rearrange("p (c j) -> p c j", j=w)

    def v_hi(s_):
        return slot_bf[s_][:, 0:4096].rearrange("p (c j) -> p c j", j=512)

    def v_lo(s_):
        return slot_bf[s_][:, 4096:8192].rearrange("p (c j) -> p c j", j=512)

    op("vector", lambda e: e.memset(ones_col[:], 1.0))
    op("vector", lambda e: e.memset(ones_row[:], 1.0))
    op("vector", lambda e: e.memset(ones_scan[:], 1.0))
    op("vector", lambda e: e.memset(carry[:], 0.0))
    v_pre = op("vector", lambda e: e.memset(fenceb[:], 0.0), inc="sV")

    dma("sync", ident[:], IDENT, inc="sLD")
    dma("sync", sele_sb[:], SELE, inc="sLD")
    dma("sync", alpha_sb[:], ALPHA, inc="sLD")
    dma("sync", biasb_sb[:], BIASB, inc="sLD")
    dma("sync", biast_sb[:], BIAST, inc="sLD")
    dma("sync", gb_row[:], GBIAS, inc="sLD")

    mk = mm_phase([
        {"mm": [(ones_row[:], alpha_sb[:])], "out": lambda b: b[0:128, 0:1]},
        {"mm": [(ones_row[:], gb_row[:])], "out": lambda b: b[0:128, 0:NE]},
    ], waits=[("sLD", cnt["sLD"]), ("sV", v_pre)], banks=[0, 1])
    e_al = evict(alphaB[:], 0, mk[0], psum_ap=ps[0][0:128, 0:1])
    e_gb = evict(gbB[:], 1, mk[1], psum_ap=ps[1][0:128, 0:NE])
    wait("vector", "sS", e_al)
    vhard(lambda e: e.tensor_scalar(tmp8[:], biast_sb[:], alphaB[:], None, ALU.mult))
    v_bvec = vhard(lambda e: e.tensor_tensor(bvec[:], biasb_sb[:], tmp8[:], ALU.add))

    kb16 = kbuf[:].rearrange("p c t -> p (c t)").bitcast(BF16)

    def src_half_ap(c, which, half):
        if c < 2:
            Wsrc = {1: SW1T, 3: SW3T, 2: SW2T}[which]
            if which in (1, 3):
                cols = slice(c * D + half * 512, c * D + (half + 1) * 512)
                return Wsrc[:, cols].rearrange("(c p) j -> p c j", p=128)
            rows = slice(c * D, (c + 1) * D)
            return Wsrc[rows, half * 512:(half + 1) * 512].rearrange("(c p) j -> p c j", p=128)
        e_ = c - 2
        Wsrc = {1: W1T, 3: W3T, 2: W2T}[which]
        rows = slice(e_ * D, (e_ + 1) * D)
        return Wsrc[rows, half * 512:(half + 1) * 512].rearrange("(c p) j -> p c j", p=128)


    store_marks = []
    v_acc_last = 0
    pre_done = 0
    pre_busy = 0
    need_pre_wait = [False]
    for tau in range(NTILE):
        if tau == 1:
            need_pre_wait[0] = True
        tok = slice(tau * TT, (tau + 1) * TT)

        d_x = dma("sync", xbuf[:], XF.rearrange("(c p) t -> p c t", p=128)[:, :, tok],
                  inc="sLD", waits=[("sV", cnt["sV"]), ("sST", pre_done)])

        ev_q = ev_k = 0
        for (Wsrc, dstbuf) in ((WqT, qbuf), (WkT, kbuf)):
            for half in range(2):
                s_ = wload([(v_f32(C, 512),
                             Wsrc.rearrange("(c p) j -> p c j", p=128)[:, :, half * 512:(half + 1) * 512])])
                mks = mm_phase(
                    [{"mm": [(v_f32(C, 512)(s_)[:, k, m * 128:(m + 1) * 128], xbuf[:, k, :])
                             for k in range(C)]} for m in range(4)],
                    waits=[("sLD", slot_ld[s_]), ("sLD", d_x)],
                    banks=[4 * half + i for i in range(4)])
                slot_mark(s_, mks[-1])
                if dstbuf is qbuf and half == 0 and store_marks:
                    wait("scalar", "sST", store_marks[-1])
                for i, m in enumerate(range(4 * half, 4 * half + 4)):
                    n_ = evict(dstbuf[:, m, :], 4 * half + i, mks[i])
                if dstbuf is qbuf:
                    ev_q = n_
                else:
                    ev_k = n_

        proj = [(Vb, ab, qbuf, ev_q), (Vt, at, qbuf, ev_q), (Wb, wb_s, kbuf, ev_k),
                (Wt, wt_s, kbuf, ev_k), (Xt, uz, xbuf, 0)]
        pb_ix = 0
        for (Wsrc, dstbuf, srcbuf, ev_need) in proj:
            s_ = wload([(v_f32(C, R), Wsrc.rearrange("(c p) r -> p c r", p=128))])
            mks = mm_phase(
                [{"mm": [(v_f32(C, R)(s_)[:, k, m * 128:(m + 1) * 128], srcbuf[:, k, :])
                         for k in range(C)]} for m in range(RC)],
                waits=[("sLD", slot_ld[s_]), ("sS", ev_need)],
                banks=[pb_ix % 8, (pb_ix + 1) % 8])
            slot_mark(s_, mks[-1])
            for m in range(RC):
                evict(dstbuf[:, m, :], (pb_ix + m) % 8, mks[m])
            pb_ix += 2
        ev_proj = cnt["sS"]

        wait("vector", "sS", ev_proj)
        scans = [(cb, wb_s, 0), (ct, wt_s, 2), (cz, uz, 4)]
        for dst, src, ci in scans:
            for rc_ in range(RC):
                op("vector", lambda e, d=dst, s=src, rc_=rc_, ci=ci:
                   e.tensor_tensor_scan(d[:, rc_, :], ones_scan[:], s[:, rc_, :],
                                        carry[:, ci + rc_:ci + rc_ + 1], ALU.mult, ALU.add),
                   inc="sV")
        wait("vector", "sV", cnt["sV"])
        for dst, src, ci in scans:
            for rc_ in range(RC):
                op("vector", lambda e, d=dst, rc_=rc_, ci=ci:
                   e.tensor_copy(carry[:, ci + rc_:ci + rc_ + 1], d[:, rc_, TT - 1:TT]), inc="sV")

        d_ict = dma("sync", ic_t[:], IC[0:1, tok], inc="sLD", waits=[("sT", cnt["sT"])])
        mkic = mm_phase([{"mm": [(ones_row[:], ic_t[:])]}],
                        waits=[("sV", v_bvec), ("sLD", d_ict)], banks=[7])
        e_ic = evict(icb[:], 7, mkic[0])

        wait("vector", "sS", e_ic)
        for rc_ in range(RC):
            op("vector", lambda e, rc_=rc_: e.tensor_tensor(uz[:, rc_, :], cz[:, rc_, :], icb[:], ALU.mult))
        for rc_ in range(RC):
            op("vector", lambda e, rc_=rc_: e.tensor_tensor(pb[:, rc_, :], ab[:, rc_, :], cb[:, rc_, :], ALU.mult))
        for rc_ in range(RC):
            op("vector", lambda e, rc_=rc_: e.tensor_tensor(pt[:, rc_, :], at[:, rc_, :], uz[:, rc_, :], ALU.mult))
        for rc_ in range(RC):
            op("vector", lambda e, rc_=rc_: e.tensor_tensor(pt[:, rc_, :], pt[:, rc_, :], ct[:, rc_, :], ALU.mult))
        v_prod = None
        for rc_ in range(RC):
            v_prod = op("vector", lambda e, rc_=rc_:
                        e.tensor_scalar(pt[:, rc_, :], pt[:, rc_, :], alphaB[:], None, ALU.mult), inc="sV")

        ubv = lambda s2: slot_f32[s2][:, 0:2048].rearrange("p (c j) -> p c j", j=1024)
        utv = lambda s2: slot_f32[s2][:, 2048:4096].rearrange("p (c j) -> p c j", j=1024)
        s_ = wload([(lambda s2: ubv(s2), UbT.rearrange("(c p) j -> p c j", p=128)),
                    (lambda s2: utv(s2), UtT.rearrange("(c p) j -> p c j", p=128))])
        mks = mm_phase(
            [{"mm": [(ubv(s_)[:, k, m * 128:(m + 1) * 128], pb[:, k, :]) for k in range(RC)]
                    + [(utv(s_)[:, k, m * 128:(m + 1) * 128], pt[:, k, :]) for k in range(RC)]}
             for m in range(C)],
            waits=[("sLD", slot_ld[s_]), ("sV", v_prod)])
        slot_mark(s_, mks[-1])
        for m in range(C):
            evict(qbuf[:, m, :], m, mks[m])
        ev_s = cnt["sS"]

        v_r1 = None
        for half in range(2):
            s_ = wload([(v_f32(C, 512),
                         WoT.rearrange("(c p) j -> p c j", p=128)[:, :, half * 512:(half + 1) * 512])])
            if tau == 0:
                mkb = mm_phase(
                    [{"mm": [(v_f32(C, 512)(s_)[:, k, m * 128:(m + 1) * 128], bvec[:, k:k + 1])
                             for k in range(C)],
                      "out": (lambda mm_: (lambda b: b[0:128, mm_:mm_ + 1]))(m)}
                     for m in range(4)],
                    waits=[("sLD", slot_ld[s_]), ("sV", v_bvec)], banks=[half] * 4)
                evict(wob[:, 4 * half:4 * half + 4], half, mkb[-1],
                      psum_ap=ps[half][0:128, 0:4])
            mks = mm_phase(
                [{"mm": [(v_f32(C, 512)(s_)[:, k, m * 128:(m + 1) * 128], qbuf[:, k, :])
                         for k in range(C)]} for m in range(4)],
                waits=[("sLD", slot_ld[s_]), ("sS", ev_s)],
                banks=[4 * half + i for i in range(4)])
            slot_mark(s_, mks[-1])
            for i, m in enumerate(range(4 * half, 4 * half + 4)):
                wait("vector", "sT", mks[i])
                n_ = op("vector", lambda e, m=m, b=4 * half + i:
                        e.tensor_tensor(r1buf[:, m, :], ps[b][:], icb[:], ALU.mult), inc="sV")
                bank_free[4 * half + i] = ("sV", n_)
            for m in range(4 * half, 4 * half + 4):
                op("vector", lambda e, m=m:
                   e.tensor_scalar(r1buf[:, m, :], r1buf[:, m, :], wob[:, m:m + 1], None, ALU.add))
            for m in range(4 * half, 4 * half + 4):
                v_r1 = op("vector", lambda e, m=m:
                          e.tensor_tensor(r1buf[:, m, :], r1buf[:, m, :], xbuf[:, m, :], ALU.add),
                          inc="sV")

        def layernorm(src, dst, v_src_ready, sq_tensor_ok=0):
            wait("scalar", "sV", v_src_ready)
            wait("scalar", "sT", sq_tensor_ok)
            s_sq = op("scalar", lambda e: e.activation(kbuf[:], src[:], AF.Square), inc="sS")
            mks2 = mm_phase(
                [{"mm": [(ones_col[:], src[:, k, :]) for k in range(C)], "out": lambda b: b[0:1, :]},
                 {"mm": [(ones_col[:], kbuf[:, k, :]) for k in range(C)], "out": lambda b: b[0:1, :]}],
                waits=[("sV", v_src_ready), ("sS", s_sq)], banks=[0, 1])
            evict(stats[0:1, ST_A:ST_A + TT], 0, mks2[0], psum_ap=ps[0][0:1, :])
            e_sq = evict(stats[0:1, ST_B:ST_B + TT], 1, mks2[1], psum_ap=ps[1][0:1, :])
            st = lambda o: stats[0:1, o:o + TT]
            wait("vector", "sS", e_sq)
            vhard(lambda e: e.tensor_scalar(st(ST_C), st(ST_A), 1.0 / D, None, ALU.mult))
            vhard(lambda e: e.tensor_scalar(st(ST_B), st(ST_B), 1.0 / D, EPS, ALU.mult, ALU.add))
            vhard(lambda e: e.tensor_tensor(st(ST_A), st(ST_C), st(ST_C), ALU.mult))
            vhard(lambda e: e.tensor_tensor(st(ST_B), st(ST_B), st(ST_A), ALU.subtract))
            v_rs = vhard(lambda e: e.reciprocal(st(ST_A), st(ST_B)))
            wait("scalar", "sV", v_rs)
            s_r0 = op("scalar", lambda e: e.activation(st(ST_D_), st(ST_A), AF.Sqrt), inc="sS")
            wait("vector", "sS", s_r0)
            vhard(lambda e: e.tensor_tensor(st(ST_C), st(ST_C), st(ST_D_), ALU.mult))
            v_nm = vhard(lambda e: e.tensor_scalar(st(ST_C), st(ST_C), -1.0, None, ALU.mult))
            mkb2 = mm_phase(
                [{"mm": [(ones_row[:], st(ST_D_))]}, {"mm": [(ones_row[:], st(ST_C))]}],
                waits=[("sV", v_nm)], banks=[2, 3])
            evict(rb[:], 2, mkb2[0])
            e_mb = evict(mb[:], 3, mkb2[1])
            wait("vector", "sS", e_mb)
            for k in range(C):
                op("vector", lambda e, k=k: e.tensor_tensor(dst[:, k, :], src[:, k, :], rb[:], ALU.mult))
            v_last = None
            for k in range(C):
                v_last = op("vector", lambda e, k=k:
                            e.tensor_tensor(dst[:, k, :], dst[:, k, :], mb[:], ALU.add), inc="sV")
            return v_last, mks2[1]

        v_y1, ln1_sq_mark = layernorm(r1buf, xbuf, v_r1)

        qb16 = qbuf[:].rearrange("p c t -> p (c t)").bitcast(BF16)
        y1h = qb16[:, 0:4096].rearrange("p (c j) -> p c j", j=512)
        y1l = qb16[:, 4096:8192].rearrange("p (c j) -> p c j", j=512)
        v_init = None
        for k in range(C):
            v_init = op("vector", lambda e, k=k:
                        e.tensor_copy(r1buf[:, k, :], xbuf[:, k, :]), inc="sV")
        s_y1h = v_y1l = 0
        if tau > 0:
            s_y1h = op("scalar", lambda e: e.copy(y1h[:, :, :], xbuf[:]), inc="sS")
            wait("vector", "sS", s_y1h)
            for k in range(C):
                v_y1l = op("vector", lambda e, k=k:
                           e.tensor_tensor(y1l[:, k, :], xbuf[:, k, :], y1h[:, k, :], ALU.subtract),
                           inc="sV")

        s_ = wload([(v_f32(C, NE), GWT.rearrange("(c p) j -> p c j", p=128))])
        mkg = mm_phase(
            [{"mm": [(v_f32(C, NE)(s_)[:, k, :], xbuf[:, k, :]) for k in range(C)],
              "out": lambda b: b[0:NE, :]}],
            waits=[("sLD", slot_ld[s_]), ("sV", v_y1)], banks=[4])
        slot_mark(s_, mkg[0])
        e_g = evict(g_sb[:], 4, mkg[0], psum_ap=ps[4][0:NE, :])
        wait("tensor", "sS", e_g)
        waits_merged("tensor", [bank_free[5]])
        tmarks = []
        for b4 in range(4):
            tmarks.append(op("tensor", lambda e, b4=b4:
                             e.transpose(ps[5][0:128, b4 * 8:(b4 + 1) * 8],
                                         g_sb[0:NE, b4 * 128:(b4 + 1) * 128],
                                         ident[0:NE, 0:NE]), inc="sT"))
        e_tm = evict(sc_tm[:], 5, tmarks[-1], psum_ap=ps[5][0:128, 0:32])
        wait("vector", "sS", e_tm)
        S8 = lambda t_, b4: t_[:, b4 * 8:(b4 + 1) * 8]
        def step(fns):
            n = None
            for f in fns:
                n = op("vector", f, inc="sV")
            wait("vector", "sV", n)
        step([lambda e, b4=b4: e.tensor_reduce(mx4[:, b4:b4 + 1], S8(sc_tm, b4), AX.X, ALU.max)
              for b4 in range(4)])
        v_nm4 = vhard(lambda e: e.tensor_scalar(nmx4[:], mx4[:], -1.0, None, ALU.mult))
        wait("scalar", "sV", v_nm4)
        s_ex = None
        for b4 in range(4):
            s_ex = op("scalar", lambda e, b4=b4:
                      e.activation(S8(ex_tm, b4), S8(sc_tm, b4), AF.Exp, bias=nmx4[:, b4:b4 + 1]),
                      inc="sS")
        wait("vector", "sS", s_ex)
        step([lambda e, b4=b4: e.tensor_reduce(sm4[:, b4:b4 + 1], S8(ex_tm, b4), AX.X, ALU.add)
              for b4 in range(4)])
        vhard(lambda e: e.reciprocal(rs4[:], sm4[:]))
        step([lambda e, b4=b4:
              e.tensor_scalar(S8(ex_tm, b4), S8(ex_tm, b4), rs4[:, b4:b4 + 1], None, ALU.mult)
              for b4 in range(4)])
        step([lambda e, b4=b4: e.tensor_tensor(S8(vv, b4), S8(ex_tm, b4), gbB[:], ALU.add)
              for b4 in range(4)])
        step([lambda e, b4=b4: e.tensor_reduce(m14[:, b4:b4 + 1], S8(vv, b4), AX.X, ALU.max)
              for b4 in range(4)])
        step([lambda e, b4=b4:
              e.tensor_scalar(S8(ge1, b4), S8(vv, b4), m14[:, b4:b4 + 1], None, ALU.is_ge)
              for b4 in range(4)])
        step([lambda e, b4=b4:
              e.tensor_scalar(S8(bigb, b4), S8(ge1, b4), -BIG, None, ALU.mult)
              for b4 in range(4)])
        step([lambda e, b4=b4: e.tensor_tensor(S8(bigb, b4), S8(vv, b4), S8(bigb, b4), ALU.add)
              for b4 in range(4)])
        step([lambda e, b4=b4: e.tensor_reduce(m24[:, b4:b4 + 1], S8(bigb, b4), AX.X, ALU.max)
              for b4 in range(4)])
        step([lambda e, b4=b4:
              e.tensor_scalar(S8(keep, b4), S8(vv, b4), m24[:, b4:b4 + 1], None, ALU.is_ge)
              for b4 in range(4)])
        v_gt = None
        for b4 in range(4):
            v_gt = op("vector", lambda e, b4=b4:
                      e.tensor_tensor(S8(gts, b4), S8(ex_tm, b4), S8(keep, b4), ALU.mult), inc="sV")
        wait("tensor", "sV", v_gt)
        waits_merged("tensor", [bank_free[6]])
        tmarks = []
        for b4 in range(4):
            tmarks.append(op("tensor", lambda e, b4=b4:
                             e.transpose(ps[6][0:NE, b4 * 128:(b4 + 1) * 128],
                                         S8(gts, b4), ident[0:128, 0:128]), inc="sT"))
        e_gf = evict(gf_sb[:], 6, tmarks[-1], psum_ap=ps[6][0:NE, 0:TT])

        kb16t = kbuf[:].rearrange("p c t -> p (c t)").bitcast(BF16)
        hh = kb16t[:, 0:4096].rearrange("p (c j) -> p c j", j=512)
        hl = kb16t[:, 4096:8192].rearrange("p (c j) -> p c j", j=512)
        pre_hi = kb16t[:, 0:4096]
        pre_lo = kb16t[:, 4096:8192]
        first0 = tau == 0
        h3buf = qbuf if first0 else xbuf

        def ffn_unit(c, which, half, rhs_waits, w2=False, extra_groups=(), extra_banks=()):
            nonlocal pre_busy
            if first0:
                s2 = wload([(lambda sl: slot_f32[sl][:, 0:4096].rearrange("p (c j) -> p c j", j=512),
                             src_half_ap(c, which, half))])
                wv = slot_f32[s2][:, 0:4096].rearrange("p (c j) -> p c j", j=512)
                rhs = h1buf if w2 else xbuf
                groups = [{"mm": [(wv[:, k, m * 128:(m + 1) * 128], rhs[:, k, :])
                                  for k in range(C)]} for m in range(4)]
            else:
                u = unit_idx(c, which, half)
                ew = [("sST", pre_done)] if need_pre_wait[0] else []
                need_pre_wait[0] = False
                s2 = wload([(lambda sl: slot_bf[sl][:, 0:4096], UHI[u]),
                            (lambda sl: slot_bf[sl][:, 4096:8192], ULO[u])],
                           extra_waits=ew)
                rhs_h, rhs_l = (hh, hl) if w2 else (y1h, y1l)
                groups = []
                for m in range(4):
                    mm = []
                    for k in range(C):
                        mm.append((v_hi(s2)[:, k, m * 128:(m + 1) * 128], rhs_h[:, k, :]))
                        mm.append((v_hi(s2)[:, k, m * 128:(m + 1) * 128], rhs_l[:, k, :]))
                        mm.append((v_lo(s2)[:, k, m * 128:(m + 1) * 128], rhs_h[:, k, :]))
                    groups.append({"mm": mm})
            banks = [4 * half + i for i in range(4)] + list(extra_banks)
            groups = groups + list(extra_groups)
            mks2 = mm_phase(groups, waits=[("sLD", slot_ld[s2])] + list(rhs_waits), banks=banks)
            if first0:
                u = unit_idx(c, which, half)
                src_ap = slot_f32[s2]
                waits_merged("scalar", [("sLD", slot_ld[s2]), ("sST", pre_busy)])
                n_hi = op("scalar", lambda e, x=src_ap: e.copy(pre_hi, x), inc="sS")
                wait("vector", "sS", n_hi)
                n_lo = op("vector", lambda e, x=src_ap:
                          e.tensor_tensor(pre_lo, x, pre_hi, ALU.subtract), inc="sV")
                waits_merged("gpsimd", [("sS", n_hi), ("sV", n_lo)])
                dma("gpsimd", UHI[u], pre_hi, inc="sST")
                pre_busy = dma("gpsimd", ULO[u], pre_lo, inc="sST")
                slot_mark(s2, mks2[-1], extra=[("sV", n_lo)])
            else:
                slot_mark(s2, mks2[-1])
            return mks2

        y1_w = [("sV", v_y1)] if first0 else [("sS", s_y1h), ("sV", v_y1l)]
        h_ready = [("sS", 0), ("sV", 0)]
        w2_last_mark = ln1_sq_mark
        geb_evs = {}

        for cyc in range(11):
            c = cyc
            cw2 = cyc - 1
            hguard = (v_y1 if first0 else v_y1l) if cyc == 0 else cnt["sV"]

            def emit_w2():
                nonlocal w2_last_mark, v_acc_last
                isexp = cw2 >= 2
                for half in range(2):
                    eg, eb = (), ()
                    if isexp and half == 0:
                        ex = cw2 - 2
                        eg = [{"mm": [(sele_sb[0:NE, ex * 128:(ex + 1) * 128], gf_sb[0:NE, :])]}]
                        eb = [7]
                    mks2 = ffn_unit(cw2, 2, half,
                                    list(h_ready) + ([("sS", e_gf)] if isexp else []),
                                    w2=True, extra_groups=eg, extra_banks=eb)
                    if isexp and half == 0:
                        geb_evs[cw2] = evict(geb[:, cw2 % 2, :], 7, mks2[4])
                    if isexp:
                        par = cw2 % 2
                        wait("vector", "sS", geb_evs[cw2])
                        for i, m in enumerate(range(4 * half, 4 * half + 4)):
                            b = 4 * half + i
                            wait("vector", "sT", mks2[i])
                            n_ = op("vector", lambda e, i=i, b=b, par=par:
                                    e.tensor_tensor(tmpc[:, i, :], ps[b][:], geb[:, par, :], ALU.mult),
                                    inc="sV")
                            bank_free[b] = ("sV", n_)
                        for i, m in enumerate(range(4 * half, 4 * half + 4)):
                            op("vector", lambda e, i=i, m=m:
                               e.tensor_tensor(r1buf[:, m, :], r1buf[:, m, :], tmpc[:, i, :], ALU.add))
                    else:
                        for i, m in enumerate(range(4 * half, 4 * half + 4)):
                            b = 4 * half + i
                            wait("vector", "sT", mks2[i])
                            if cw2 == 0:
                                wait("vector", "sV", v_init)
                            n_ = op("vector", lambda e, m=m, b=b:
                                    e.tensor_tensor(r1buf[:, m, :], r1buf[:, m, :], ps[b][:], ALU.add),
                                    inc="sV")
                            bank_free[b] = ("sV", n_)
                    if half == 1:
                        w2_last_mark = mks2[3]
                v_acc_last = cnt["sV"]

            def emit_w13():
                for half in range(2):
                    mks2 = ffn_unit(c, 1, half, y1_w)
                    for i, m in enumerate(range(4 * half, 4 * half + 4)):
                        b = 4 * half + i
                        waits_merged("scalar", [("sT", mks2[i]), ("sV", hguard)])
                        n_ = op("scalar", lambda e, m=m, b=b:
                                e.activation(h1buf[:, m, :], ps[b][:], AF.Silu), inc="sS")
                        bank_free[b] = ("sS", n_)
                for half in range(2):
                    mks2 = ffn_unit(c, 3, half, y1_w)
                    for i, m in enumerate(range(4 * half, 4 * half + 4)):
                        evict(h3buf[:, m, :], 4 * half + i, mks2[i], waits=[("sV", hguard)])

            if first0:
                if cw2 >= 0:
                    emit_w2()
                if c <= 9:
                    emit_w13()
            else:
                if c <= 9:
                    emit_w13()
                if cw2 >= 0:
                    emit_w2()

            if c <= 9:
                ev_h13 = cnt["sS"]
                wait("vector", "sS", ev_h13)
                v_h = None
                for m in range(C):
                    v_h = op("vector", lambda e, m=m, hb=h3buf:
                             e.tensor_tensor(h1buf[:, m, :], h1buf[:, m, :], hb[:, m, :], ALU.mult),
                             inc="sV")
                if first0:
                    h_ready = [("sV", v_h), ("sV", v_h)]
                else:
                    waits_merged("scalar", [("sT", w2_last_mark), ("sV", v_h)])
                    s_hh = op("scalar", lambda e: e.copy(hh[:, :, :], h1buf[:]), inc="sS")
                    wait("vector", "sS", s_hh)
                    v_hl = None
                    for m in range(C):
                        v_hl = op("vector", lambda e, m=m:
                                  e.tensor_tensor(hl[:, m, :], h1buf[:, m, :], hh[:, m, :], ALU.subtract),
                                  inc="sV")
                    h_ready = [("sS", s_hh), ("sV", v_hl)]

        if first0:
            pre_done = cnt["sST"]
        if tau == 0:
            wait("scalar", "sST", pre_done)
        v_out, _ = layernorm(r1buf, qbuf, v_acc_last, sq_tensor_ok=w2_last_mark)

        smark = dma("gpsimd", YF.rearrange("(c p) t -> p c t", p=128)[:, :, tok], qbuf[:],
                    inc="sST", waits=[("sV", v_out)])
        store_marks.append(smark)

    wait("gpsimd", "sST", cnt["sST"])

    with nc.Block() as block:
        @block.sync
        def _(e):
            for fn in stream["sync"]:
                fn(e)

        @block.tensor
        def _(e):
            for fn in stream["tensor"]:
                fn(e)

        @block.vector
        def _(e):
            for fn in stream["vector"]:
                fn(e)

        @block.scalar
        def _(e):
            for fn in stream["scalar"]:
                fn(e)

        @block.gpsimd
        def _(e):
            for fn in stream["gpsimd"]:
                fn(e)

    return nc


def _sele():
    s = np.zeros((NE, NE * 128), dtype=np.float32)
    for e in range(NE):
        s[e, e * 128:(e + 1) * 128] = 1.0
    return s


def kernel(**inputs):
    g = {k: np.asarray(v) for k, v in inputs.items()}
    x = g["L0_attn_x"]
    B = x.shape[0]

    def cc(a):
        return np.ascontiguousarray(a.astype(np.float32, copy=False))

    shared = {
        "WqT": cc(g["L0_attn_Wq"].T), "WkT": cc(g["L0_attn_Wk"].T), "WoT": cc(g["L0_attn_Wo"].T),
        "Vb": cc(g["L0_attn_V_b"]), "Wb": cc(g["L0_attn_W_b"]),
        "Vt": cc(g["L0_attn_V_t"]), "Wt": cc(g["L0_attn_W_t"]), "Xt": cc(g["L0_attn_X_t"]),
        "UbT": cc(g["L0_attn_U_b"].T), "UtT": cc(g["L0_attn_U_t"].T),
        "BIASB": cc(g["L0_attn_bias_b"].reshape(C, 128).T),
        "BIAST": cc(g["L0_attn_bias_t"].reshape(C, 128).T),
        "ALPHA": cc(g["L0_attn_alpha"].reshape(1, 1)),
        "GWT": cc(g["L0_ffn_gate_w"].T),
        "GBIAS": cc(g["L0_ffn_gate_bias"].reshape(1, NE)),
        "W1T": cc(g["L0_ffn_w1"].transpose(0, 2, 1).reshape(NE * D, D)),
        "W3T": cc(g["L0_ffn_w3"].transpose(0, 2, 1).reshape(NE * D, D)),
        "W2T": cc(g["L0_ffn_w2"].transpose(0, 2, 1).reshape(NE * D, D)),
        "SW1T": cc(g["L0_ffn_sw1"].T), "SW3T": cc(g["L0_ffn_sw3"].T),
        "SW2T": cc(g["L0_ffn_sw2"].T),
        "IC": (1.0 / np.arange(1, T + 1, dtype=np.float64)).astype(np.float32).reshape(1, T),
        "IDENT": np.eye(128, dtype=np.float32),
        "SELE": _sele(),
    }
    in_maps = []
    for b in range(B):
        m = dict(shared)
        m["XF"] = cc(x[b].T)
        in_maps.append(m)

    nc = build_nc()
    res = run_bass_kernel_spmd(nc, in_maps, core_ids=list(range(B)))
    out = np.stack([np.asarray(r["YF"]).T for r in res.results])
    return np.ascontiguousarray(out.astype(np.float32))
